# version 02
# baseline (speedup 1.0000x reference)
import numpy as np
from contextlib import ExitStack

import concourse.bass as bass
import concourse.mybir as mybir
from concourse.bass_utils import run_bass_kernel_spmd

F32 = mybir.dt.float32
BF16 = mybir.dt.bfloat16
AF = mybir.ActivationFunctionType
ALU = mybir.AluOpType
AX = mybir.AxisListType

T = 4096
D = 1024
NP = 8
CT = 8
FT = 16
DT = 8
E9 = 9
NCH = 8
TT = 32
CH = 4
TCH = 1024

_DEBUG_TAPS = False


class _Sem:
    def __init__(self, st, nc, name, step=1):
        self.h = st.enter_context(nc.semaphore(name))
        self.n = 0
        self.step = step

    def inc(self, instr):
        instr.then_inc(self.h, self.step)
        self.n += self.step
        return self.n


class _Waiter:
    def __init__(self):
        self.seen = {}

    def __call__(self, eng, sem, val=None):
        v = sem.n if val is None else val
        if v <= 0:
            return
        k = (id(eng), id(sem))
        if self.seen.get(k, 0) < v:
            eng.wait_ge(sem.h, v)
            self.seen[k] = v


class _Auto:

    def __init__(self, eng, sem, waiter, skip=("wait_ge", "dma_start")):
        self._e = eng
        self._s = sem
        self._W = waiter
        self._skip = set(skip)

    def __getattr__(self, name):
        f = getattr(self._e, name)
        if name in self._skip or name.startswith("_"):
            return f

        def g(*a, **k):
            self._W(self._e, self._s, self._s.n)
            ins = f(*a, **k)
            self._s.n += 1
            ins.then_inc(self._s.h, 1)
            return self._s.n

        return g


def build_program(dbg=False, phase_limit=99, pairs=NP, do_attn=True,
                  do_norms=True):
    nc = bass.Bass("TRN2", target_bir_lowering=False, debug=False)
    tn, sl, ve, gp, sy = nc.tensor, nc.scalar, nc.vector, nc.gpsimd, nc.sync

    x_d = nc.dram_tensor("x", [T, D], F32, kind="ExternalInput")
    wq_d = nc.dram_tensor("wq_t", [NP, 128, CT, 128], F32, kind="ExternalInput")
    wk_d = nc.dram_tensor("wk_t", [NP, 128, CT, 128], F32, kind="ExternalInput")
    wv_d = nc.dram_tensor("wv_t", [NP, 128, CT, 128], F32, kind="ExternalInput")
    wo_d = nc.dram_tensor("wo_t", [128, CT, DT, 128], F32, kind="ExternalInput")
    bq_d = nc.dram_tensor("bq_t", [128, 16], F32, kind="ExternalInput")
    bk_d = nc.dram_tensor("bk_t", [128, 16], F32, kind="ExternalInput")
    bv_d = nc.dram_tensor("bv_t", [128, 16], F32, kind="ExternalInput")
    bo_d = nc.dram_tensor("bo_t", [128, 8], F32, kind="ExternalInput")
    s4_d = nc.dram_tensor("s4", [1, 4], F32, kind="ExternalInput")
    rw1_d = nc.dram_tensor("rw1_t", [E9, 128, FT, CT, 128], F32, kind="ExternalInput")
    rw2_d = nc.dram_tensor("rw2_t", [E9, 128, FT, DT, 128], F32, kind="ExternalInput")
    rb1_d = nc.dram_tensor("rb1_t", [128, E9, FT], F32, kind="ExternalInput")
    rb2_d = nc.dram_tensor("rb2s", [E9, D], F32, kind="ExternalInput")
    rwr_d = nc.dram_tensor("rwr_t", [128, CT, 8], F32, kind="ExternalInput")
    rbr_d = nc.dram_tensor("rbr_t", [8, 1], F32, kind="ExternalInput")
    sel_d = nc.dram_tensor("sel9", [E9, 8 * 128], F32, kind="ExternalInput")
    out_d = nc.dram_tensor("out", [T, D], F32, kind="ExternalOutput")
    ao_d = nc.dram_tensor("aout_scr", [DT, 128, T], F32, kind="Internal")
    taps = {}
    if dbg:
        taps["d_xnT"] = nc.dram_tensor("d_xnT", [128, CT, T], BF16, kind="ExternalOutput")
        taps["d_q0"] = nc.dram_tensor("d_q0", [65, T], BF16, kind="ExternalOutput")
        taps["d_k0"] = nc.dram_tensor("d_k0", [65, T], BF16, kind="ExternalOutput")
        taps["d_ctxT"] = nc.dram_tensor("d_ctxT", [128, CT, T], BF16, kind="ExternalOutput")
        taps["d_aout"] = nc.dram_tensor("d_aout", [DT, 128, T], F32, kind="ExternalOutput")
        taps["d_W"] = nc.dram_tensor("d_W", [E9, T], BF16, kind="ExternalOutput")
        taps["d_acc0"] = nc.dram_tensor("d_acc0", [128, DT, TCH], F32, kind="ExternalOutput")

    W = _Waiter()
    with ExitStack() as st:
        e = st.enter_context
        tS = _Sem(st, nc, "tS")
        sS = _Sem(st, nc, "sS")
        vS = _Sem(st, nc, "vS")
        gS = _Sem(st, nc, "gS")
        dX = [_Sem(st, nc, f"dX{i}", 16) for i in range(2)]
        dR = _Sem(st, nc, "dR", 16)
        dAO = [_Sem(st, nc, f"dAO{i}", 16) for i in range(2)]
        dACC = _Sem(st, nc, "dACC", 16)
        dOut = [_Sem(st, nc, f"dOut{i}", 16) for i in range(2)]
        dDBG = _Sem(st, nc, "dDBG", 16)
        dG = _Sem(st, nc, "dG", 16)
        dRW2 = _Sem(st, nc, "dRW2", 16)
        dW1 = [_Sem(st, nc, f"dW1{i}", 16) for i in range(2)]
        sync_sems = dX + [dR] + dAO + [dACC] + dOut + [dDBG]
        gp_sems = [dG, dRW2] + dW1

        def wait_all_dma(eng):
            for s_ in sync_sems + gp_sems:
                W(eng, s_, s_.n)

        def dma(q, qsem, out, in_):
            q.dma_start(out=out, in_=in_).then_inc(qsem.h, 16)
            qsem.n += 16
            return qsem.n

        vv = _Auto(ve, vS, W)
        ss = _Auto(sl, sS, W)
        gg = _Auto(gp, gS, W)

        xnT = e(nc.sbuf_tensor("xnT", [128, CT, T], BF16))
        ident = e(nc.sbuf_tensor("ident", [128, 128], BF16))
        identf = e(nc.sbuf_tensor("identf", [128, 128], F32))
        eps = e(nc.sbuf_tensor("eps", [128, 1], F32))
        ones_r128 = e(nc.sbuf_tensor("ones_r128", [1, 128], F32))
        ones_c64 = e(nc.sbuf_tensor("ones_c64", [64, 1], BF16))
        ones_all = e(nc.sbuf_tensor("ones_all", [128, 64], F32))
        ones_512 = e(nc.sbuf_tensor("ones_512", [1, 512], F32))
        onesb = e(nc.sbuf_tensor("onesb", [1, 128], BF16))
        scol = e(nc.sbuf_tensor("scol", [128, 4], F32))
        s_sb = e(nc.sbuf_tensor("s_sb", [1, 4], F32))
        bq_sb = e(nc.sbuf_tensor("bq_sb", [128, 16], F32))
        bk_sb = e(nc.sbuf_tensor("bk_sb", [128, 16], F32))
        bv_sb = e(nc.sbuf_tensor("bv_sb", [128, 16], F32))
        bo_sb = e(nc.sbuf_tensor("bo_sb", [128, 8], F32))
        rbr_sb = e(nc.sbuf_tensor("rbr_sb", [8, 1], F32))
        W_aug = e(nc.sbuf_tensor("W_aug", [E9, T], BF16))

        for dst, src in [(s_sb, s4_d), (bq_sb, bq_d), (bk_sb, bk_d), (bv_sb, bv_d),
                         (bo_sb, bo_d), (rbr_sb, rbr_d)]:
            dma(gp, dG, dst[:], src[:])
        d_pre = dG.n
        gg.memset(ident[:], 0.0)
        gg.affine_select(out=ident[:], in_=ident[:], compare_op=ALU.not_equal,
                         fill=1.0, base=0, pattern=[[-1, 128]], channel_multiplier=1)
        gg.memset(identf[:], 0.0)
        gg.affine_select(out=identf[:], in_=identf[:], compare_op=ALU.not_equal,
                         fill=1.0, base=0, pattern=[[-1, 128]], channel_multiplier=1)
        gg.memset(eps[:], 1e-5)
        gg.memset(ones_r128[:], 1.0)
        gg.memset(ones_c64[:], 1.0)
        gg.memset(ones_all[:], 1.0)
        gg.memset(ones_512[:], 1.0)
        gg.memset(onesb[:], 1.0)
        g_pre = gg.memset(W_aug[:], 1.0)

        with ExitStack() as st0:
            e0 = st0.enter_context
            xt2 = e0(nc.sbuf_tensor("p0_x", [128, 2, D], F32))
            xnb = e0(nc.sbuf_tensor("p0_xn", [128, 2, D], BF16))
            xsq = e0(nc.sbuf_tensor("p0_xsq", [128, D], F32))
            st0_ = e0(nc.sbuf_tensor("p0_stat", [128, 8], F32))
            pbf = e0(nc.psum_tensor("p0_pbf", [128, 1024], BF16))
            ppre = e0(nc.psum_tensor("p0_pre", [128, 8], F32))

            W(tn, dG, d_pre)
            W(tn, gS, g_pre)
            for i in range(4):
                ins = tn.matmul(ppre[:, i:i + 1], ones_r128[:], s_sb[0:1, i:i + 1],
                                start=True, stop=True)
            t_pre = tS.inc(ins)
            W(ve, tS, t_pre)
            W(ve, dG, d_pre)
            vv.tensor_copy(out=scol[:], in_=ppre[:, 0:4])
            vv.tensor_scalar(out=scol[:, 0:1], in0=scol[:, 0:1], scalar1=0.125,
                             scalar2=None, op0=ALU.mult)
            v_pre = vv.tensor_scalar(out=bq_sb[:], in0=bq_sb[:], scalar1=0.125,
                                     scalar2=None, op0=ALU.mult)

            v_xn = [0] * TT
            v_cp = [0] * TT
            d_x = [0] * TT
            d_x[0] = dma(sy, dX[0], xt2[:, 0, :], x_d[0:128, :])
            d_x[1] = dma(sy, dX[1], xt2[:, 1, :], x_d[128:256, :])
            v_stat = [0] * TT
            for i in range(TT):
                xt = xt2[:, i % 2, :]
                W(sl, dX[i % 2], d_x[i])
                if i >= 1:
                    W(sl, vS, v_stat[i - 1])
                s_sqr = ss.activation(out=xsq[:], in_=xt, func=AF.Square)
                W(ve, dX[i % 2], d_x[i])
                W(ve, vS, v_pre)
                vv.reduce_sum(out=st0_[:, 0:1], in_=xt, axis=AX.X)
                W(ve, sS, s_sqr)
                vv.reduce_sum(out=st0_[:, 1:2], in_=xsq[:], axis=AX.X)
                vv.tensor_tensor(out=st0_[:, 2:3], in0=st0_[:, 0:1],
                                 in1=st0_[:, 0:1], op=ALU.mult)
                vv.scalar_tensor_tensor(out=st0_[:, 3:4], in0=st0_[:, 2:3],
                                        scalar=-1.0 / D, in1=st0_[:, 1:2],
                                        op0=ALU.mult, op1=ALU.add)
                v_st = vv.tensor_scalar(out=st0_[:, 4:5], in0=st0_[:, 0:1],
                                        scalar1=-1.0 / D, scalar2=None, op0=ALU.mult)
                v_stat[i] = v_st
                W(sl, vS, v_st)
                ss.activation(out=st0_[:, 5:6], in_=st0_[:, 3:4], func=AF.Ln,
                              bias=eps[:], scale=1.0 / D)
                s_r = ss.activation(out=st0_[:, 6:7], in_=st0_[:, 5:6],
                                    func=AF.Exp, scale=-0.5)
                W(ve, sS, s_r)
                v_xn[i] = vv.tensor_scalar(out=xnb[:, i % 2, :], in0=xt,
                                           scalar1=st0_[:, 4:5],
                                           scalar2=st0_[:, 6:7],
                                           op0=ALU.add, op1=ALU.mult)
                if i + 2 < TT:
                    W(sy, vS, v_xn[i])
                    d_x[i + 2] = dma(sy, dX[i % 2], xt2[:, i % 2, :],
                                     x_d[128 * (i + 2):128 * (i + 3), :])
                W(tn, vS, v_xn[i])
                if i >= 1:
                    W(tn, vS, v_cp[i - 1])
                for j in range(CT):
                    ins = tn.transpose(pbf[:, 128 * j:128 * (j + 1)],
                                       xnb[:, i % 2, 128 * j:128 * (j + 1)], ident[:])
                t_tr = tS.inc(ins)
                W(ve, tS, t_tr)
                v_cp[i] = vv.tensor_copy(
                    out=xnT[:, :, 128 * i:128 * (i + 1)],
                    in_=pbf[:].rearrange("p (c t) -> p c t", c=CT))
            v_ln_done = v_cp[TT - 1]

        if dbg:
            W(sy, vS, v_ln_done)
            W(sy, dDBG, dDBG.n) or dma(sy, dDBG, taps["d_xnT"][:], xnT[:])
        if phase_limit < 1:
            wait_all_dma(sy)
            wait_all_dma(gp)
            return nc

        with ExitStack() as stA:
            eA = stA.enter_context
            ctxT = eA(nc.sbuf_tensor("ctxT", [128, CT, T], BF16))

            with ExitStack() as stAA:
                eAA = stAA.enter_context
                QT = [eAA(nc.sbuf_tensor(f"QT{h}", [65, T], BF16)) for h in range(2)]
                KT = [eAA(nc.sbuf_tensor(f"KT{h}", [65, T], BF16)) for h in range(2)]
                VT = eAA(nc.sbuf_tensor("VT", [128, T], BF16))
                Vaug = eAA(nc.sbuf_tensor("Vaug", [128, TT, 66], BF16))
                P2 = eAA(nc.sbuf_tensor("P2", [128, 2, 1024], BF16))
                wbuf = eAA(nc.sbuf_tensor("wbuf", [128, 3, CT, 128], BF16))
                sqq = eAA(nc.sbuf_tensor("sqq", [64, T], BF16))
                recro = eAA(nc.sbuf_tensor("recro", [128, 512], F32))
                stA_ = eAA(nc.sbuf_tensor("stA", [128, 16], F32))
                pS = [eAA(nc.psum_tensor(f"pS{i}", [128, 1024], F32)) for i in range(2)]
                pC = eAA(nc.psum_tensor("pC", [65, 512], F32))
                pB = eAA(nc.psum_tensor("pB", [64, 512], F32))
                pAbf = [eAA(nc.psum_tensor(f"pAbf{i}", [128, 512], BF16))
                        for i in range(2)]

                W(gp, gS, g_pre)
                W(gp, vS, vS.n)
                W(gp, sS, sS.n)
                W(gp, tS, tS.n)
                wait_all_dma(gp)
                gg.memset(QT[0][64:65, :], 1.0)
                gg.memset(QT[1][64:65, :], 1.0)
                g_rows = gg.memset(Vaug[:, :, 64:65], 1.0)

                slots = [(pS[0], 0), (pS[0], 512), (pS[1], 0), (pS[1], 512)]

                t_projdone = [0] * NP
                prev_exp = None
                prev_tPV = None
                v_ctx_prev = 0
                t_bc_prev = 0
                d_ctx_prev = 0
                d_dbgA = 0

                for p in range(pairs):
                    h0, h1 = 2 * p, 2 * p + 1
                    if p >= 1:
                        W(gp, tS, t_projdone[p - 1])
                        W(gp, tS, tS.n)
                    for mi, wd in enumerate([wq_d, wk_d, wv_d]):
                        dma(gp, dG, wbuf[:, mi, :, :], wd[p])
                    d_w = dG.n
                    W(ve, dG, d_w)
                    W(ve, vS, v_pre)
                    for mi in range(3):
                        v_wsc = vv.tensor_scalar(out=wbuf[:, mi, :, :],
                                                 in0=wbuf[:, mi, :, :],
                                                 scalar1=scol[:, mi:mi + 1],
                                                 scalar2=None, op0=ALU.mult)

                    W(tn, vS, v_wsc)
                    W(tn, vS, v_ln_done)
                    W(tn, gS, g_rows)
                    if dbg and d_dbgA:
                        W(tn, dDBG, d_dbgA)
                    slot_tick = [sS.n] * 4
                    kch = 0

                    def proj(mat, lo_dst, hi_dst, b_sb, lo_wait=0, hi_wait=0):
                        nonlocal kch
                        for m in range(NCH):
                            ps, off = slots[kch % 4]
                            W(tn, sS, slot_tick[kch % 4])
                            for ci in range(CT):
                                ins = tn.matmul(
                                    ps[:, off:off + 512],
                                    wbuf[:, mat, ci, :],
                                    xnT[:, ci, 512 * m:512 * (m + 1)],
                                    start=(ci == 0), stop=(ci == CT - 1))
                            t_mm = tS.inc(ins)
                            W(sl, tS, t_mm)
                            if lo_wait:
                                W(sl, dR, lo_wait)
                            if hi_wait:
                                W(sl, dR, hi_wait)
                            ss.activation(out=lo_dst[0:64, 512 * m:512 * (m + 1)],
                                          in_=ps[0:64, off:off + 512], func=AF.Identity,
                                          bias=b_sb[0:64, h0:h0 + 1], scale=1.0)
                            slot_tick[kch % 4] = ss.activation(
                                out=hi_dst[64:128, 512 * m:512 * (m + 1)],
                                in_=ps[64:128, off:off + 512], func=AF.Identity,
                                bias=b_sb[64:128, h1:h1 + 1], scale=1.0)
                            kch += 1
                        return slot_tick[(kch - 1) % 4]

                    s_q = proj(0, QT[0], VT, bq_sb, hi_wait=d_ctx_prev)
                    W(sy, sS, s_q)
                    d_q1 = dma(sy, dR, QT[1][0:64, :], VT[64:128, :])
                    s_k = proj(1, KT[0], VT, bk_sb, hi_wait=d_q1)
                    W(sy, sS, s_k)
                    d_k1 = dma(sy, dR, KT[1][0:64, :], VT[64:128, :])
                    proj(2, VT, VT, bv_sb, lo_wait=d_ctx_prev, hi_wait=d_k1)
                    t_projdone[p] = tS.n

                    for hh in range(2 if do_norms else 0):
                        qt, kt = QT[hh], KT[hh]
                        nslot = 0
                        nticks = [("s", sS.n)] * 4

                        def nwait(i):
                            kind, val = nticks[i]
                            W(tn, sS if kind == "s" else vS, val)

                        for which, tt in ((0, qt), (1, kt)):
                            if hh == 1:
                                W(ve, dR, d_k1)
                            W(ve, sS, s_q if which == 0 else s_k)
                            v_sq = vv.tensor_tensor(out=sqq[:], in0=tt[0:64, :],
                                                    in1=tt[0:64, :], op=ALU.mult)
                            W(tn, vS, v_sq)
                            for m in range(NCH):
                                ps, off = slots[nslot % 4]
                                nwait(nslot % 4)
                                ins = tn.matmul(ps[0:1, off:off + 512], ones_c64[:],
                                                sqq[:, 512 * m:512 * (m + 1)],
                                                start=True, stop=True)
                                t_n = tS.inc(ins)
                                W(ve, tS, t_n)
                                nticks[nslot % 4] = ("v", vv.reduce_max(
                                    out=stA_[0:1, m:m + 1],
                                    in_=ps[0:1, off:off + 512], axis=AX.X))
                                nslot += 1
                            vv.reduce_max(out=stA_[0:1, 8 + which:9 + which],
                                          in_=stA_[0:1, 0:8], axis=AX.X)
                        v_c2 = vv.tensor_tensor(out=stA_[0:1, 10:11],
                                                in0=stA_[0:1, 8:9],
                                                in1=stA_[0:1, 9:10], op=ALU.mult)
                        W(sl, vS, v_c2)
                        ss.activation(out=stA_[0:1, 11:12], in_=stA_[0:1, 10:11],
                                      func=AF.Ln, bias=eps[0:1, :], scale=1.0)
                        s_ln = ss.activation(out=stA_[0:1, 12:13],
                                             in_=stA_[0:1, 11:12],
                                             func=AF.Exp, scale=0.5)
                        W(ve, sS, s_ln)
                        v_negc = vv.tensor_scalar(out=stA_[0:1, 13:14],
                                                  in0=stA_[0:1, 12:13],
                                                  scalar1=-1.0, scalar2=None,
                                                  op0=ALU.mult)
                        W(tn, vS, v_negc)
                        for m in range(NCH):
                            ps, off = slots[nslot % 4]
                            nwait(nslot % 4)
                            ins = tn.matmul(ps[64:65, off:off + 512], stA_[0:1, 13:14],
                                            ones_512[:], start=True, stop=True,
                                            tile_position=(0, 64))
                            t_n = tS.inc(ins)
                            W(sl, tS, t_n)
                            nticks[nslot % 4] = ("s", ss.activation(
                                out=kt[64:65, 512 * m:512 * (m + 1)],
                                in_=ps[64:65, off:off + 512],
                                func=AF.Identity, scale=1.0))
                            nslot += 1
                        s_negc_done = nticks[(nslot - 1) % 4][1]

                        idsl = ident[0:64, 0:64] if hh == 0 else ident[64:128, 64:128]
                        v_vcp = [("v", vS.n)] * 2
                        for g in range(4 if do_attn else 0):
                            W(tn, vS, v_vcp[g % 2][1])
                            for j in range(8):
                                kti = 8 * g + j
                                ins = tn.transpose(
                                    pAbf[g % 2][:, 64 * j:64 * (j + 1)],
                                    VT[64 * hh:64 * (hh + 1), 128 * kti:128 * (kti + 1)],
                                    idsl)
                            t_v = tS.inc(ins)
                            W(ve, tS, t_v)
                            v_vcp[g % 2] = ("v", vv.tensor_copy(
                                out=Vaug[:, 8 * g:8 * (g + 1), 0:64],
                                in_=pAbf[g % 2][:]
                                .rearrange("p (a b) -> p a b", a=8)))
                        v_vaug = v_vcp[1][1]

                        W(tn, sS, s_negc_done)
                        W(tn, vS, v_vaug)
                        for q in range(NCH if do_attn else 0):
                            s_exp = [0] * 16
                            t_PV = [0] * 16
                            for j in range(16):
                                if j >= 2:
                                    W(tn, sS, s_exp[j - 2])
                                elif prev_exp is not None:
                                    W(tn, sS, prev_exp[14 + j])
                                for half in range(2):
                                    ktile = 2 * j + half
                                    ins = tn.matmul(
                                        pS[j % 2][:, 512 * half:512 * (half + 1)],
                                        kt[:, 128 * ktile:128 * (ktile + 1)],
                                        qt[:, 512 * q:512 * (q + 1)],
                                        start=True, stop=True)
                                t_S = tS.inc(ins)
                                if j >= 1:
                                    jj = j - 1
                                    W(tn, sS, s_exp[jj])
                                    if jj == 0:
                                        W(tn, vS, v_ctx_prev)
                                    for half in range(2):
                                        ktile = 2 * jj + half
                                        ins = tn.matmul(
                                            pC[:], Vaug[:, ktile, 0:65],
                                            P2[:, jj % 2, 512 * half:512 * (half + 1)],
                                            start=(ktile == 0), stop=False)
                                    t_PV[jj] = tS.inc(ins)
                                W(sl, tS, t_S)
                                if j >= 2:
                                    W(sl, tS, t_PV[j - 2])
                                elif prev_tPV is not None:
                                    W(sl, tS, prev_tPV[14 + j])
                                s_exp[j] = ss.activation(out=P2[:, j % 2, :],
                                                         in_=pS[j % 2][:], func=AF.Exp)
                            W(tn, sS, s_exp[15])
                            for half in range(2):
                                ins = tn.matmul(pC[:], Vaug[:, 30 + half, 0:65],
                                                P2[:, 1, 512 * half:512 * (half + 1)],
                                                start=False, stop=(half == 1))
                            t_pvl = tS.inc(ins)
                            W(ve, tS, t_pvl)
                            W(ve, tS, t_bc_prev)
                            v_rec = vv.reciprocal(out=recro[64:65, :],
                                                  in_=pC[64:65, :])
                            W(tn, vS, v_rec)
                            ins = tn.matmul(pB[:], ones_all[64:65, 0:64],
                                            recro[64:65, :], start=True, stop=True)
                            t_bc_prev = tS.inc(ins)
                            W(ve, tS, t_bc_prev)
                            vv.tensor_copy(out=recro[0:64, :], in_=pB[:])
                            if hh == 0:
                                cdst = ctxT[0:64, p, 512 * q:512 * (q + 1)]
                            else:
                                cdst = VT[0:64, 512 * q:512 * (q + 1)]
                            v_ctx_prev = vv.tensor_tensor(out=cdst, in0=pC[0:64, :],
                                                          in1=recro[0:64, :],
                                                          op=ALU.mult)
                            prev_exp = s_exp
                            prev_tPV = t_PV
                        if hh == 1:
                            W(sy, vS, v_ctx_prev)
                            d_ctx_prev = dma(sy, dR, ctxT[64:128, p, :], VT[0:64, :])
                    if dbg and p == 0:
                        W(sy, sS, sS.n)
                        W(sy, vS, vS.n)
                        W(sy, dDBG, dDBG.n) or dma(sy, dDBG, taps["d_q0"][:], QT[0][:])
                        W(sy, dDBG, dDBG.n)
                        d_dbgA = dma(sy, dDBG, taps["d_k0"][:], KT[0][:])
                d_ctx_done = dR.n
                t_attn_done = tS.n

            if dbg:
                W(sy, vS, vS.n)
                W(sy, dR, d_ctx_done)
                W(sy, dDBG, dDBG.n) or dma(sy, dDBG, taps["d_ctxT"][:], ctxT[:])
            if phase_limit < 2:
                wait_all_dma(sy)
                wait_all_dma(gp)
                return nc

            with ExitStack() as stB:
                eB = stB.enter_context
                wo_sb = eB(nc.sbuf_tensor("wo_sb", [128, CT, DT, 128], BF16))
                rwr_sb = eB(nc.sbuf_tensor("rwr_sb", [128, CT, 8], BF16))
                stg = eB(nc.sbuf_tensor("stg", [128, 2, T], F32))
                rre = eB(nc.sbuf_tensor("rre", [8, 512], F32))
                rt_sb = eB(nc.sbuf_tensor("rt_sb", [128, 8], F32))
                rs_sb = eB(nc.sbuf_tensor("rs_sb", [128, 8], F32))
                w_sb = eB(nc.sbuf_tensor("w_sb", [128, 8], F32))
                rstat = eB(nc.sbuf_tensor("rstat", [128, 4], F32))
                pO = [eB(nc.psum_tensor(f"pO{i}", [128, 512], F32)) for i in range(2)]
                pR = eB(nc.psum_tensor("pR", [8, 512], F32))
                pRT = eB(nc.psum_tensor("pRT", [128, 8], F32))
                pWT = eB(nc.psum_tensor("pWT", [8, 128], F32))

                W(gp, tS, t_attn_done)
                W(gp, vS, vS.n)
                W(gp, sS, sS.n)
                wait_all_dma(gp)
                dma(gp, dG, wo_sb[:], wo_d[:])
                dma(gp, dG, rwr_sb[:], rwr_d[:])
                d_wo = dG.n
                W(ve, dG, d_wo)
                v_wo = vv.tensor_scalar(
                    out=wo_sb[:].rearrange("p a b c -> p (a b c)"),
                    in0=wo_sb[:].rearrange("p a b c -> p (a b c)"),
                    scalar1=scol[:, 3:4], scalar2=None, op0=ALU.mult)

                d_ao = [0] * DT
                d_aodbg = [0] * DT
                s_ev = [sS.n] * 2
                W(tn, vS, v_wo)
                W(tn, dR, d_ctx_done)
                for d in range(DT):
                    for m in range(NCH):
                        W(tn, sS, s_ev[m % 2])
                        for ci in range(CT):
                            ins = tn.matmul(pO[m % 2][:], wo_sb[:, ci, d, :],
                                            ctxT[:, ci, 512 * m:512 * (m + 1)],
                                            start=(ci == 0), stop=(ci == CT - 1))
                        t_mm = tS.inc(ins)
                        W(sl, tS, t_mm)
                        if d >= 2:
                            W(sl, dAO[d % 2], d_ao[d - 2])
                            if dbg:
                                W(sl, dDBG, d_aodbg[d - 2])
                        s_ev[m % 2] = ss.activation(
                            out=stg[:, d % 2, 512 * m:512 * (m + 1)],
                            in_=pO[m % 2][:], func=AF.Identity,
                            bias=bo_sb[:, d:d + 1], scale=1.0)
                    W(sy, sS, s_ev[(NCH - 1) % 2])
                    d_ao[d] = dma(sy, dAO[d % 2], ao_d[d], stg[:, d % 2, :])
                    if dbg:
                        W(sy, dDBG, dDBG.n)
                        d_aodbg[d] = dma(sy, dDBG, taps["d_aout"][d],
                                         stg[:, d % 2, :])
                d_ao_done = (dAO[0].n, dAO[1].n)

                s_rr = sS.n
                t_rt_last = tS.n
                W(tn, dG, d_wo)
                for m in range(NCH):
                    W(tn, sS, s_rr)
                    for ci in range(CT):
                        ins = tn.matmul(pR[:], rwr_sb[:, ci, :],
                                        xnT[:, ci, 512 * m:512 * (m + 1)],
                                        start=(ci == 0), stop=(ci == CT - 1))
                    t_rm = tS.inc(ins)
                    W(sl, tS, t_rm)
                    W(sl, tS, t_rt_last)
                    s_rr = ss.activation(out=rre[:], in_=pR[:], func=AF.Identity,
                                         bias=rbr_sb[:], scale=1.0)
                    for k in range(4):
                        stt = 4 * m + k
                        W(tn, sS, s_rr)
                        ins = tn.transpose(pRT[:], rre[:, 128 * k:128 * (k + 1)],
                                           identf[0:8, 0:8])
                        t_rt_last = tS.inc(ins)
                        W(ve, tS, t_rt_last)
                        vv.tensor_copy(out=rt_sb[:], in_=pRT[:])
                        vv.reduce_max(out=rstat[:, 0:1], in_=rt_sb[:], axis=AX.X)
                        v_rm = vv.tensor_scalar(out=rstat[:, 1:2], in0=rstat[:, 0:1],
                                                scalar1=-1.0, scalar2=None,
                                                op0=ALU.mult)
                        W(sl, vS, v_rm)
                        s_re = ss.activation(out=rs_sb[:], in_=rt_sb[:],
                                             func=AF.Exp, bias=rstat[:, 1:2],
                                             scale=1.0)
                        W(ve, sS, s_re)
                        vv.reduce_sum(out=rstat[:, 2:3], in_=rs_sb[:], axis=AX.X)
                        vv.reciprocal(out=rstat[:, 3:4], in_=rstat[:, 2:3])
                        vv.tensor_scalar(out=rs_sb[:], in0=rs_sb[:],
                                         scalar1=rstat[:, 3:4], scalar2=None,
                                         op0=ALU.mult)
                        vv.reduce_max(out=rstat[:, 0:1], in_=rs_sb[:], axis=AX.X)
                        vv.tensor_scalar(out=rt_sb[:], in0=rs_sb[:],
                                         scalar1=rstat[:, 0:1], scalar2=None,
                                         op0=ALU.is_equal)
                        vv.tensor_tensor(out=w_sb[:], in0=rs_sb[:], in1=rt_sb[:],
                                         op=ALU.mult)
                        vv.tensor_tensor(out=w_sb[:], in0=rs_sb[:], in1=w_sb[:],
                                         op=ALU.subtract)
                        vv.reduce_max(out=rstat[:, 1:2], in_=w_sb[:], axis=AX.X)
                        vv.tensor_scalar(out=w_sb[:], in0=w_sb[:],
                                         scalar1=rstat[:, 1:2], scalar2=None,
                                         op0=ALU.is_equal)
                        vv.tensor_tensor(out=rt_sb[:], in0=rt_sb[:], in1=w_sb[:],
                                         op=ALU.add)
                        v_w = vv.tensor_tensor(out=w_sb[:], in0=rs_sb[:],
                                               in1=rt_sb[:], op=ALU.mult)
                        W(tn, vS, v_w)
                        ins = tn.transpose(pWT[:], w_sb[:], identf[:])
                        t_wt = tS.inc(ins)
                        W(ve, tS, t_wt)
                        vv.tensor_copy(out=W_aug[0:8, 128 * stt:128 * (stt + 1)],
                                       in_=pWT[:])
                v_W_done = vS.n
                if dbg:
                    W(sy, vS, v_W_done)
                    W(sy, dDBG, dDBG.n) or dma(sy, dDBG, taps["d_W"][:], W_aug[:])

        if phase_limit < 3:
            wait_all_dma(sy)
            wait_all_dma(gp)
            return nc

        with ExitStack() as stC:
            eC = stC.enter_context
            acc = eC(nc.sbuf_tensor("acc", [128, DT, TCH], F32))
            rb1_sb = eC(nc.sbuf_tensor("rb1_sb", [128, E9, FT], F32))
            rb2_sb = eC(nc.sbuf_tensor("rb2_sb", [E9, D], BF16))
            sel_sb = eC(nc.sbuf_tensor("sel_sb", [E9, 8 * 128], BF16))
            H = eC(nc.sbuf_tensor("H", [128, FT, TCH], BF16))
            rw2_sb = eC(nc.sbuf_tensor("rw2_sb", [128, FT, DT, 128], BF16))
            rw1r = eC(nc.sbuf_tensor("rw1r", [128, 2, 2, CT, 128], BF16))
            bcw = eC(nc.sbuf_tensor("bcw", [128, TCH], BF16))
            xt2c = eC(nc.sbuf_tensor("xt2c", [128, 2, D], F32))
            ostg = eC(nc.sbuf_tensor("ostg", [128, 2, D], F32))
            pH = [eC(nc.psum_tensor(f"pH{i}", [128, 1024], F32)) for i in range(2)]
            pOc = [eC(nc.psum_tensor(f"pOc{i}", [128, 512], F32)) for i in range(2)]
            pW2 = eC(nc.psum_tensor("pW2", [128, 512], F32))
            pD = eC(nc.psum_tensor("pD", [128, 512], F32))

            NU = FT // 2
            uglob = 0
            t_u_used = {}
            d_u = {}
            t_rw2_used = 0
            t_pD_done = 0
            v_pD = [0, 0]
            s_bc_last = 0
            v_hw_last = 0

            W(gp, tS, tS.n)
            W(gp, vS, vS.n)
            W(gp, sS, sS.n)
            wait_all_dma(gp)
            dma(gp, dG, rb1_sb[:], rb1_d[:])
            dma(gp, dG, rb2_sb[:], rb2_d[:])
            dma(gp, dG, sel_sb[:], sel_d[:])
            d_cpre = dG.n
            W(tn, vS, v_W_done)
            W(tn, dG, d_cpre)
            W(sl, dG, d_cpre)
            t_phC = tS.n
            for c in range(CH):
                W(sy, tS, t_pD_done if c else t_phC)
                W(sy, vS, v_W_done)
                if dbg:
                    W(sy, dDBG, dDBG.n)
                W(sy, dAO[0], d_ao_done[0])
                W(sy, dAO[1], d_ao_done[1])
                for d in range(DT):
                    dma(sy, dACC, acc[:, d, :], ao_d[d][:, TCH * c:TCH * (c + 1)])
                d_acc = dACC.n

                s_gel = {}
                v_hw = {}
                v_evac = [vS.n] * 2
                for ei in range(E9):
                    W(gp, tS, t_rw2_used)
                    d_rw2 = dma(gp, dRW2, rw2_sb[:], rw2_d[ei])

                    def issue_u(uu_):
                        ui_ = uglob + uu_
                        if ui_ - 2 in t_u_used:
                            W(gp, tS, t_u_used[ui_ - 2])
                        d_u[ui_] = dma(gp, dW1[ui_ % 2], rw1r[:, ui_ % 2, :, :, :],
                                       rw1_d[ei][:, 2 * uu_:2 * (uu_ + 1), :, :])

                    issue_u(0)
                    issue_u(1)
                    if ei < 8:
                        W(tn, vS, v_hw_last)
                        W(tn, vS, v_pD[1])
                        W(tn, sS, s_bc_last)
                        for half in range(2):
                            W(tn, sS, s_bc_last)
                            ins = tn.matmul(pW2[:],
                                            sel_sb[:, 128 * ei:128 * (ei + 1)],
                                            W_aug[:, TCH * c + 512 * half:
                                                  TCH * c + 512 * (half + 1)],
                                            start=True, stop=True)
                            t_bcw = tS.inc(ins)
                            W(sl, tS, t_bcw)
                            s_bc_last = ss.activation(
                                out=bcw[:, 512 * half:512 * (half + 1)],
                                in_=pW2[:], func=AF.Identity, scale=1.0)
                        W(tn, sS, s_bc_last)

                    for fj in range(FT):
                        ui = uglob + fj // 2
                        W(tn, dW1[ui % 2], d_u[ui])
                        W(tn, sS, s_gel.get(fj - 2, 0))
                        for half in range(2):
                            for ci in range(CT):
                                ins = tn.matmul(
                                    pH[fj % 2][:, 512 * half:512 * (half + 1)],
                                    rw1r[:, ui % 2, fj % 2, ci, :],
                                    xnT[:, ci, TCH * c + 512 * half:
                                        TCH * c + 512 * (half + 1)],
                                    start=(ci == 0), stop=(ci == CT - 1))
                        t_fm = tS.inc(ins)
                        t_u_used[ui] = t_fm
                        if fj % 2 == 1 and fj // 2 + 2 < NU:
                            issue_u(fj // 2 + 2)
                        W(sl, tS, t_fm)
                        s_gel[fj] = ss.activation(out=H[:, fj, :], in_=pH[fj % 2][:],
                                                  func=AF.Gelu,
                                                  bias=rb1_sb[:, ei, fj:fj + 1],
                                                  scale=1.0)
                        if ei < 8:
                            W(ve, sS, s_gel[fj])
                            v_hw[fj] = vv.tensor_tensor(out=H[:, fj, :],
                                                        in0=H[:, fj, :],
                                                        in1=bcw[:], op=ALU.mult)
                    if ei < 8:
                        v_hw_last = v_hw[FT - 1]
                        W(tn, vS, v_hw_last)
                    else:
                        W(tn, sS, s_gel[FT - 1])

                    W(tn, dRW2, d_rw2)
                    for d in range(DT):
                        for half in range(2):
                            W(tn, vS, v_evac[half])
                            for fj in range(FT):
                                ins = tn.matmul(
                                    pOc[half][:], rw2_sb[:, fj, d, :],
                                    H[:, fj, 512 * half:512 * (half + 1)],
                                    start=(fj == 0), stop=(fj == FT - 1))
                            t_sm = tS.inc(ins)
                            W(ve, tS, t_sm)
                            if d == 0 and half == 0:
                                W(ve, dACC, d_acc)
                            v_evac[half] = vv.tensor_tensor(
                                out=acc[:, d, 512 * half:512 * (half + 1)],
                                in0=acc[:, d, 512 * half:512 * (half + 1)],
                                in1=pOc[half][:], op=ALU.add)
                    t_rw2_used = tS.n
                    uglob += NU

                for d in range(DT):
                    for half in range(2):
                        W(tn, vS, v_evac[half])
                        ins = tn.matmul(pOc[half][:], rb2_sb[:, 128 * d:128 * (d + 1)],
                                        W_aug[:, TCH * c + 512 * half:
                                              TCH * c + 512 * (half + 1)],
                                        start=True, stop=True)
                        t_b2 = tS.inc(ins)
                        W(ve, tS, t_b2)
                        v_evac[half] = vv.tensor_tensor(
                            out=acc[:, d, 512 * half:512 * (half + 1)],
                            in0=acc[:, d, 512 * half:512 * (half + 1)],
                            in1=pOc[half][:], op=ALU.add)
                v_accdone = vS.n
                if dbg and c == 0:
                    W(sy, vS, v_accdone)
                    W(sy, dDBG, dDBG.n) or dma(sy, dDBG, taps["d_acc0"][:], acc[:])

                v_addfull = [vS.n] * 2
                d_og = [dOut[0].n, dOut[1].n]
                for stt in range(8):
                    buf = stt % 2
                    row = 128 * (8 * c + stt)
                    W(sy, vS, v_addfull[buf])
                    d_xt = dma(sy, dX[buf], xt2c[:, buf, :], x_d[row:row + 128, :])
                    for r in range(2):
                        ps = pD if r == 0 else pW2
                        W(tn, vS, v_pD[r])
                        if r == 1:
                            W(tn, sS, s_bc_last)
                        W(tn, vS, v_accdone)
                        for dd in range(4):
                            d = 4 * r + dd
                            ins = tn.transpose(ps[:, 128 * dd:128 * (dd + 1)],
                                               acc[:, d, 128 * stt:128 * (stt + 1)],
                                               identf[:])
                        t_tp = tS.inc(ins)
                        W(ve, tS, t_tp)
                        W(ve, dX[buf], d_xt)
                        W(ve, dOut[buf], d_og[buf])
                        v_pD[r] = vv.tensor_tensor(
                            out=ostg[:, buf, 512 * r:512 * (r + 1)],
                            in0=xt2c[:, buf, 512 * r:512 * (r + 1)],
                            in1=ps[:], op=ALU.add)
                    v_addfull[buf] = vS.n
                    W(sy, vS, v_addfull[buf])
                    d_og[buf] = dma(sy, dOut[buf], out_d[row:row + 128, :],
                                    ostg[:, buf, :])
                t_pD_done = tS.n

        wait_all_dma(sy)
        wait_all_dma(gp)
    return nc


def _host_layouts(inp):
    f = np.float32

    def a(k):
        return np.asarray(inp[k], dtype=f)

    wq = a("L0_attn_w_q"); wk = a("L0_attn_w_k")
    wv = a("L0_attn_w_v"); wo = a("L0_attn_w_o")

    def wtile(w):
        return np.ascontiguousarray(
            w.T.reshape(8, 128, 8, 128).transpose(2, 1, 0, 3))

    def btile(b):
        t = b.reshape(16, 64).T
        return np.ascontiguousarray(np.concatenate([t, t], axis=0))

    rw1 = a("L0_ffn_rw1"); sw1 = a("L0_ffn_sw1")
    rw2 = a("L0_ffn_rw2"); sw2 = a("L0_ffn_sw2")
    rb1 = a("L0_ffn_rb1"); sb1 = a("L0_ffn_sb1")
    rb2 = a("L0_ffn_rb2"); sb2 = a("L0_ffn_sb2")
    rw1s = np.concatenate([rw1, sw1[None]], axis=0)
    rw2s = np.concatenate([rw2, sw2[None]], axis=0)
    rb1s = np.concatenate([rb1, sb1[None]], axis=0)
    rb2s = np.concatenate([rb2, sb2[None]], axis=0)

    shared = {
        "wq_t": wtile(wq), "wk_t": wtile(wk), "wv_t": wtile(wv),
        "wo_t": np.ascontiguousarray(
            wo.T.reshape(8, 128, 8, 128).transpose(1, 0, 2, 3)),
        "bq_t": btile(a("L0_attn_b_q")), "bk_t": btile(a("L0_attn_b_k")),
        "bv_t": btile(a("L0_attn_b_v")),
        "bo_t": np.ascontiguousarray(a("L0_attn_b_o").reshape(8, 128).T),
        "s4": np.stack([a("L0_attn_s_q").reshape(()), a("L0_attn_s_k").reshape(()),
                        a("L0_attn_s_v").reshape(()), a("L0_attn_s_o").reshape(())]
                       ).reshape(1, 4).astype(f),
        "rw1_t": np.ascontiguousarray(
            rw1s.reshape(9, 8, 128, 16, 128).transpose(0, 2, 3, 1, 4)),
        "rw2_t": np.ascontiguousarray(
            rw2s.reshape(9, 16, 128, 8, 128).transpose(0, 2, 1, 3, 4)),
        "rb1_t": np.ascontiguousarray(rb1s.reshape(9, 16, 128).transpose(2, 0, 1)),
        "rb2s": np.ascontiguousarray(rb2s),
        "rwr_t": np.ascontiguousarray(
            a("L0_ffn_router_w").reshape(8, 128, 8).transpose(1, 0, 2)),
        "rbr_t": np.ascontiguousarray(a("L0_ffn_router_b").reshape(8, 1)),
        "sel9": np.ascontiguousarray(
            np.repeat(np.eye(9, 8, dtype=f), 128, axis=1)),
    }
    x = np.asarray(inp["L0_attn_x"], dtype=f)
    return shared, x


def kernel(**inp):
    shared, x = _host_layouts(inp)
    nc = build_program(dbg=_DEBUG_TAPS)
    in_maps = []
    for b in range(8):
        m = dict(shared)
        m["x"] = np.ascontiguousarray(x[b])
        in_maps.append(m)
    res = run_bass_kernel_spmd(nc, in_maps, core_ids=list(range(8)))
    out = np.stack([np.asarray(r["out"], dtype=np.float32) for r in res.results],
                   axis=0)
    if _DEBUG_TAPS:
        kernel._last_results = res
    return out
